# version 01
# baseline (speedup 1.0000x reference)
import contextlib
import numpy as np
import ml_dtypes
import concourse.bass as bass
import concourse.mybir as mybir
from concourse.bass_utils import run_bass_kernel_spmd

F32 = mybir.dt.float32
BF16 = mybir.dt.bfloat16
AF = mybir.ActivationFunctionType
ALU = mybir.AluOpType
AX = mybir.AxisListType
BF16NP = ml_dtypes.bfloat16

D = 2048
SEQ = 4096
B = 2
NCORES = 8
TPC = 1024
NT = 8
NDC = 16
EPS = 1e-5
DENOM = float(np.float32(np.sqrt(np.float32(2048.0)) / np.float32(4096.0)))


class _Sem:
    __slots__ = ("h", "v")


class Ctx:
    def __init__(self, name="TRN2"):
        self.nc = bass.Bass(name, target_bir_lowering=False)
        self.st = contextlib.ExitStack()

    def sem(self, name):
        s = _Sem()
        s.h = self.st.enter_context(self.nc.semaphore(name))
        s.v = 0
        return s

    def sbuf(self, name, shape, dt, stack=None):
        return (stack or self.st).enter_context(self.nc.sbuf_tensor(name, shape, dt))

    def psum(self, name, shape, dt=F32):
        return self.st.enter_context(self.nc.psum_tensor(name, shape, dt))

    def close(self):
        self.st.close()
        return self.nc


def inc(instr, sem, n=1):
    instr.then_inc(sem.h, n)
    sem.v += n


def fence(eng, instr, sem):
    inc(instr, sem, 1)
    eng.wait_ge(sem.h, sem.v)



def emit_ln(ctx, blk, x_dram, xnT, id_bf, psT, pfx, xnbf=None, id_ld=None):
    xa = ctx.sbuf(pfx + "xa", [128, 2, D], F32)
    sq = ctx.sbuf(pfx + "sq", [128, D], F32)
    stt = ctx.sbuf(pfx + "st", [128, 8], F32)
    xnt = ctx.sbuf(pfx + "xnt", [128, 2, D], BF16)

    ld_x = ctx.sem(pfx + "ld_x")
    var_done = ctx.sem(pfx + "var")
    sqrt_done = ctx.sem(pfx + "sqrt")
    xn_done = ctx.sem(pfx + "xn")
    pe_tp = ctx.sem(pfx + "petp")
    tp_done = ctx.sem(pfx + "tp")
    vf = ctx.sem(pfx + "vf")

    for tt in range(NT):
        buf = tt % 2

        @blk.sync
        def _(s, tt=tt, buf=buf):
            if tt >= 2:
                s.wait_ge(xn_done.h, tt - 1)
            i = s.dma_start(out=xa[:, buf, :], in_=x_dram[tt * 128:(tt + 1) * 128, :])
            inc(i, ld_x, 16)

        @blk.vector
        def _(v, buf=buf, ldv=ld_x.v):
            v.wait_ge(ld_x.h, ldv)
            i = v.reduce_sum(stt[:, 0:1], xa[:, buf, :], axis=AX.X)
            fence(v, i, vf)
            i = v.scalar_tensor_tensor(sq[:, :], xa[:, buf, :], 1.0, xa[:, buf, :],
                                       ALU.mult, ALU.mult, accum_out=stt[:, 1:2])
            fence(v, i, vf)
            i = v.tensor_scalar(stt[:, 2:3], stt[:, 0:1], 1.0 / D, None, ALU.mult)
            fence(v, i, vf)
            v.tensor_scalar(stt[:, 4:5], stt[:, 0:1], -1.0 / D, None, ALU.mult)
            i = v.tensor_tensor(stt[:, 3:4], stt[:, 2:3], stt[:, 2:3], ALU.mult)
            fence(v, i, vf)
            i = v.tensor_scalar(stt[:, 5:6], stt[:, 1:2], 1.0 / D, None, ALU.mult)
            fence(v, i, vf)
            i = v.tensor_tensor(stt[:, 5:6], stt[:, 5:6], stt[:, 3:4], ALU.subtract)
            fence(v, i, vf)
            i = v.tensor_scalar(stt[:, 5:6], stt[:, 5:6], EPS, None, ALU.add)
            inc(i, var_done, 1)

        @blk.scalar
        def _(sc, tt=tt):
            sc.wait_ge(var_done.h, tt + 1)
            i = sc.activation(stt[:, 6:7], stt[:, 5:6], AF.Sqrt)
            inc(i, sqrt_done, 1)

        @blk.vector
        def _(v, tt=tt, buf=buf):
            v.wait_ge(sqrt_done.h, tt + 1)
            i = v.reciprocal(stt[:, 7:8], stt[:, 6:7])
            fence(v, i, vf)
            if tt >= 2:
                v.wait_ge(pe_tp.h, 4 * (tt - 1))
            i = v.tensor_scalar(xnt[:, buf, :], xa[:, buf, :], stt[:, 4:5], stt[:, 7:8],
                                ALU.add, ALU.mult)
            if xnbf is not None:
                i = v.tensor_copy(xnbf[:, tt, :], xnt[:, buf, :])
            inc(i, xn_done, 1)

        @blk.tensor
        def _(t, tt=tt, buf=buf):
            t.wait_ge(xn_done.h, tt + 1)
            if tt == 0 and id_ld is not None:
                t.wait_ge(id_ld[0].h, id_ld[1])
            for g in range(4):
                gid = 4 * tt + g
                if gid >= 2:
                    t.wait_ge(tp_done.h, gid - 1)
                pst = psT[gid % 2]
                for k_ in range(4):
                    dc = g * 4 + k_
                    mm = t.matmul(pst[:, k_, :], xnt[:, buf, dc * 128:(dc + 1) * 128],
                                  id_bf[:, :], start=True, stop=True)
                inc(mm, pe_tp, 1)

        @blk.vector
        def _(v, tt=tt):
            for g in range(4):
                gid = 4 * tt + g
                v.wait_ge(pe_tp.h, gid + 1)
                pst = psT[gid % 2]
                i = v.tensor_copy(xnT[:, g * 4:(g + 1) * 4, tt * 128:(tt + 1) * 128],
                                  pst[:, :, :])
                inc(i, tp_done, 1)

    return xn_done, tp_done



def build_l0pre():
    ctx = Ctx()
    nc = ctx.nc

    x = nc.dram_tensor("x", [TPC, D], F32, kind="ExternalInput").ap()
    wqT = nc.dram_tensor("wqT", [D, D], F32, kind="ExternalInput").ap()
    wkT = nc.dram_tensor("wkT", [D, D], F32, kind="ExternalInput").ap()
    wvT = nc.dram_tensor("wvT", [D, D], F32, kind="ExternalInput").ap()
    ident = nc.dram_tensor("ident", [128, 128], F32, kind="ExternalInput").ap()

    xn_bf = nc.dram_tensor("xn_bf", [TPC, D], BF16, kind="ExternalOutput").ap()
    qfm = nc.dram_tensor("qfm", [D, TPC], BF16, kind="ExternalOutput").ap()
    ufm = nc.dram_tensor("ufm", [D, TPC], BF16, kind="ExternalOutput").ap()
    totals = nc.dram_tensor("totals", [128, NDC], F32, kind="ExternalOutput").ap()

    xnT = ctx.sbuf("xnT", [128, NDC, TPC], BF16)
    xnbf = ctx.sbuf("xnbf", [128, NT, D], BF16)
    id_bf = ctx.sbuf("id_bf", [128, 128], BF16)
    w_sb = ctx.sbuf("w_sb", [128, 2, 3, NDC, 128], BF16)
    qe = ctx.sbuf("qe", [128, 2, TPC], BF16)
    ue = ctx.sbuf("ue", [128, 2, TPC], BF16)
    ke = ctx.sbuf("ke", [128, TPC], F32)
    tot = ctx.sbuf("tot", [128, NDC], F32)

    psT = [ctx.psum(f"psT{i}", [128, 4, 128]) for i in range(2)]
    psq = [ctx.psum(f"psq{i}", [128, 512]) for i in range(2)]
    psk = [ctx.psum(f"psk{i}", [128, 512]) for i in range(2)]
    psv = [ctx.psum(f"psv{i}", [128, 512]) for i in range(2)]

    id_ld = ctx.sem("id_ld")
    wv_ld = ctx.sem("wv_ld")
    qkv_pe = ctx.sem("qkv_pe")
    qkv_ve = ctx.sem("qkv_ve")
    qs_done = ctx.sem("qs_done")
    us_done = ctx.sem("us_done")
    xs_done = ctx.sem("xs_done")

    blk = ctx.st.enter_context(nc.Block())

    @blk.gpsimd
    def _(g):
        i = g.dma_start(out=id_bf[:, :], in_=ident[:, :])
        inc(i, id_ld, 16)

    xn_done, tp_done = emit_ln(ctx, blk, x, xnT, id_bf, psT, "l0", xnbf=xnbf,
                               id_ld=(id_ld, 16))

    @blk.sync
    def _(s):
        s.wait_ge(xn_done.h, NT)
        i = s.dma_start(out=xn_bf.rearrange("(tt p) d -> p tt d", p=128), in_=xnbf[:, :, :])
        inc(i, xs_done, 16)

    for ec in range(NDC):
        buf = ec % 2

        @blk.gpsimd
        def _(g, ec=ec, buf=buf):
            if ec >= 2:
                g.wait_ge(qkv_ve.h, ec - 1)
            for wi, w in enumerate((wqT, wkT, wvT)):
                i = g.dma_start(out=w_sb[:, buf, wi, :, :],
                                in_=w[:, ec * 128:(ec + 1) * 128]
                                .rearrange("(dc p) e -> p dc e", p=128))
                inc(i, wv_ld, 16)

        @blk.tensor
        def _(t, ec=ec, buf=buf, ldv=wv_ld.v):
            t.wait_ge(wv_ld.h, ldv)
            if ec == 0:
                t.wait_ge(tp_done.h, 4 * NT)
            else:
                t.wait_ge(qkv_ve.h, ec)
            for half in range(2):
                for wi, pst in ((0, psq), (1, psk), (2, psv)):
                    ps = pst[half]
                    for dc in range(NDC):
                        mm = t.matmul(ps[:, :], w_sb[:, buf, wi, dc, :],
                                      xnT[:, dc, half * 512:(half + 1) * 512],
                                      start=(dc == 0), stop=(dc == NDC - 1))
                    inc(mm, qkv_pe, 1)

        @blk.vector
        def _(v, ec=ec, buf=buf, pev=qkv_pe.v):
            base = pev - 6
            if ec >= 2:
                v.wait_ge(qs_done.h, 16 * (ec - 1))
                v.wait_ge(us_done.h, 16 * (ec - 1))
            for half in range(2):
                v.wait_ge(qkv_pe.h, base + 3 * half + 1)
                v.tensor_copy(qe[:, buf, half * 512:(half + 1) * 512], psq[half][:, :])
                v.wait_ge(qkv_pe.h, base + 3 * half + 2)
                v.tensor_copy(ke[:, half * 512:(half + 1) * 512], psk[half][:, :])
                v.wait_ge(qkv_pe.h, base + 3 * half + 3)
                v.tensor_tensor(ue[:, buf, half * 512:(half + 1) * 512],
                                ke[:, half * 512:(half + 1) * 512], psv[half][:, :],
                                ALU.mult)
            i = v.reduce_sum(tot[:, ec:ec + 1], ue[:, buf, :], axis=AX.X)
            inc(i, qkv_ve, 1)

        @blk.sync
        def _(s, ec=ec, buf=buf, vev=qkv_ve.v):
            s.wait_ge(qkv_ve.h, vev)
            i = s.dma_start(out=qfm[ec * 128:(ec + 1) * 128, :], in_=qe[:, buf, :])
            inc(i, qs_done, 16)
            i = s.dma_start(out=ufm[ec * 128:(ec + 1) * 128, :], in_=ue[:, buf, :])
            inc(i, us_done, 16)

    @blk.sync
    def _(s):
        s.wait_ge(qkv_ve.h, NDC)
        i = s.dma_start(out=totals[:, :], in_=tot[:, :])
        inc(i, qs_done, 16)
        s.wait_ge(qs_done.h, qs_done.v)
        s.wait_ge(us_done.h, us_done.v)
        s.wait_ge(xs_done.h, 16)

    return ctx.close()



def build_l0post():
    ctx = Ctx()
    nc = ctx.nc

    qfm = nc.dram_tensor("qfm", [D, TPC], BF16, kind="ExternalInput").ap()
    ufm = nc.dram_tensor("ufm", [D, TPC], BF16, kind="ExternalInput").ap()
    gtot = nc.dram_tensor("gtot", [NCORES, 128, NDC], F32, kind="ExternalInput").ap()
    sel = nc.dram_tensor("sel", [128, NCORES], F32, kind="ExternalInput").ap()
    woT = nc.dram_tensor("woT", [D, D], F32, kind="ExternalInput").ap()

    s_bf = nc.dram_tensor("s_bf", [TPC, D], BF16, kind="ExternalOutput").ap()

    q_sb = ctx.sbuf("q_sb", [128, NDC, TPC], BF16)
    u_sb = ctx.sbuf("u_sb", [128, NDC, TPC], BF16)
    vh_sb = ctx.sbuf("vh_sb", [128, NDC, TPC], BF16)
    gt_sb = ctx.sbuf("gt_sb", [128, NCORES, NDC], F32)
    sel_sb = ctx.sbuf("sel_sb", [128, NCORES], F32)
    offs = ctx.sbuf("offs", [128, NDC], F32)
    kvt = ctx.sbuf("kvt", [128, TPC], F32)
    wo_sb = ctx.sbuf("wo_sb", [128, 2, NDC, 512], BF16)
    se_sb = ctx.sbuf("se_sb", [128, NT, D], BF16)

    pss = [ctx.psum(f"pss{i}", [128, 512]) for i in range(4)]

    ld_q = ctx.sem("ld_q")
    ld_w = ctx.sem("ld_w")
    vh_done = ctx.sem("vh")
    s_pe = ctx.sem("s_pe")
    s_ve = ctx.sem("s_ve")
    s_st = ctx.sem("s_st")
    p_vf = ctx.sem("p_vf")

    blk = ctx.st.enter_context(nc.Block())

    @blk.sync
    def _(s):
        i = s.dma_start(out=q_sb[:, :, :], in_=qfm.rearrange("(ec p) t -> p ec t", p=128))
        inc(i, ld_q, 16)
        i = s.dma_start(out=u_sb[:, :, :], in_=ufm.rearrange("(ec p) t -> p ec t", p=128))
        inc(i, ld_q, 16)
        i = s.dma_start(out=gt_sb[:, :, :], in_=gtot.rearrange("g p e -> p g e"))
        inc(i, ld_q, 16)
        i = s.dma_start(out=sel_sb[:, :], in_=sel[:, :])
        inc(i, ld_q, 16)

    @blk.vector
    def _(v):
        v.wait_ge(ld_q.h, 64)
        v.memset(offs[:, :], 0.0)
        for g in range(NCORES):
            i = v.scalar_tensor_tensor(offs[:, :], gt_sb[:, g, :], sel_sb[:, g:g + 1],
                                       offs[:, :], ALU.mult, ALU.add)
        fence(v, i, p_vf)
        for ec in range(NDC):
            i = v.tensor_tensor_scan(kvt[:, :], u_sb[:, ec, :], u_sb[:, ec, :],
                                     offs[:, ec:ec + 1], ALU.add, ALU.bypass)
            fence(v, i, p_vf)
            i = v.scalar_tensor_tensor(vh_sb[:, ec, :], kvt[:, :], DENOM,
                                       q_sb[:, ec, :], ALU.mult, ALU.mult)
            inc(i, vh_done, 1)

    for fb in range(4):
        buf = fb % 2

        @blk.gpsimd
        def _(g, fb=fb, buf=buf):
            if fb >= 2:
                g.wait_ge(s_ve.h, 8 * (fb - 1))
            i = g.dma_start(out=wo_sb[:, buf, :, :],
                            in_=woT[:, fb * 512:(fb + 1) * 512]
                            .rearrange("(ec p) f -> p ec f", p=128))
            inc(i, ld_w, 16)

        @blk.tensor
        def _(t, fb=fb, buf=buf, ldv=ld_w.v):
            t.wait_ge(ld_w.h, ldv)
            if fb == 0:
                t.wait_ge(vh_done.h, NDC)
            for tt in range(NT):
                ps = pss[tt % 4]
                pid = fb * NT + tt
                if pid >= 4:
                    t.wait_ge(s_ve.h, pid - 3)
                for ec in range(NDC):
                    mm = t.matmul(ps[:, :], vh_sb[:, ec, tt * 128:(tt + 1) * 128],
                                  wo_sb[:, buf, ec, :],
                                  start=(ec == 0), stop=(ec == NDC - 1))
                inc(mm, s_pe, 1)

        @blk.vector
        def _(v, fb=fb):
            for tt in range(NT):
                pid = fb * NT + tt
                v.wait_ge(s_pe.h, pid + 1)
                ps = pss[tt % 4]
                i = v.tensor_copy(se_sb[:, tt, fb * 512:(fb + 1) * 512], ps[:, :])
                inc(i, s_ve, 1)

    @blk.sync
    def _(s):
        s.wait_ge(s_ve.h, 32)
        i = s.dma_start(out=s_bf.rearrange("(tt p) d -> p tt d", p=128), in_=se_sb[:, :, :])
        inc(i, s_st, 16)
        s.wait_ge(s_st.h, 16)

    return ctx.close()



def emit_tp(ctx, blk, src_dram, xnT, id_bf, psA2, pfx, id_ld, stack=None, after=None):
    xs = ctx.sbuf(pfx + "xs", [128, 2, D], BF16, stack=stack)
    ld = ctx.sem(pfx + "ld")
    pe_tp = ctx.sem(pfx + "petp")
    tp_done = ctx.sem(pfx + "tp")
    psv = [p.rearrange("p (a b) -> p a b", a=4) for p in psA2]

    for tt in range(NT):
        buf = tt % 2

        @blk.sync
        def _(s, tt=tt, buf=buf):
            if tt == 0 and after:
                for (sm, val) in after:
                    s.wait_ge(sm.h, val)
            if tt >= 2:
                s.wait_ge(pe_tp.h, 4 * (tt - 1))
            i = s.dma_start(out=xs[:, buf, :], in_=src_dram[tt * 128:(tt + 1) * 128, :])
            inc(i, ld, 16)

        @blk.tensor
        def _(t, tt=tt, buf=buf, ldv=ld.v):
            t.wait_ge(ld.h, ldv)
            if tt == 0 and id_ld is not None:
                t.wait_ge(id_ld[0].h, id_ld[1])
            for g in range(4):
                gid = 4 * tt + g
                if gid >= 2:
                    t.wait_ge(tp_done.h, gid - 1)
                pst = psv[gid % 2]
                for k_ in range(4):
                    dc = g * 4 + k_
                    mm = t.matmul(pst[:, k_, :], xs[:, buf, dc * 128:(dc + 1) * 128],
                                  id_bf[:, :], start=True, stop=True)
                inc(mm, pe_tp, 1)

        @blk.vector
        def _(v, tt=tt):
            for g in range(4):
                gid = 4 * tt + g
                v.wait_ge(pe_tp.h, gid + 1)
                i = v.tensor_copy(xnT[:, g * 4:(g + 1) * 4, tt * 128:(tt + 1) * 128],
                                  psv[gid % 2][:, :, :])
                inc(i, tp_done, 1)

    return tp_done



def emit_experts(ctx, blk, xnT, acc, gates, pexps, ps, sems, nic, nhc, pfx, stack=None,
                 after=None):
    I = 128 * nic
    w1c = ctx.sbuf(pfx + "w1c", [128, 2, NDC, 128], BF16, stack=stack)
    w3c = ctx.sbuf(pfx + "w3c", [128, 2, NDC, 128], BF16, stack=stack)
    wC = ctx.sbuf(pfx + "wC", [128, nic, D], BF16, stack=stack)
    a_sb = ctx.sbuf(pfx + "a", [128, 2, nic, TPC], BF16, stack=stack)
    silu_t = ctx.sbuf(pfx + "silu", [128, 2, TPC], BF16, stack=stack)

    wch, a_pe, sa, a_done, w2_ld, b_pe, b_ve = (
        sems[k] for k in ("wch", "a_pe", "sa", "a_done", "w2_ld", "b_pe", "b_ve"))

    mm_apex = {}
    mm_adone = {}
    mm_sa = {}
    mm_bpe = {}
    mm_bve = {}

    npx = len(pexps)
    for pi, (w1_ap, w3_ap, w2_ap, gi) in enumerate(pexps):
        abuf = pi % 2

        for hc in range(nic):
            cbuf = hc % 2

            @blk.gpsimd
            def _(g, pi=pi, hc=hc, cbuf=cbuf):
                if pi == 0 and hc == 0 and after:
                    for (sm, val) in after:
                        g.wait_ge(sm.h, val)
                if hc >= 2:
                    key = (pi, hc - 2)
                elif pi >= 1:
                    pl = nic - 1 if (nic - 1) % 2 == hc % 2 else nic - 2
                    key = (pi - 1, pl)
                else:
                    key = None
                if key is not None:
                    g.wait_ge(a_pe.h, mm_apex[key])
                i = g.dma_start(out=w1c[:, cbuf, :, :],
                                in_=w1_ap[:, hc * 128:(hc + 1) * 128]
                                .rearrange("(dc p) h -> p dc h", p=128))
                inc(i, wch, 16)
                i = g.dma_start(out=w3c[:, cbuf, :, :],
                                in_=w3_ap[:, hc * 128:(hc + 1) * 128]
                                .rearrange("(dc p) h -> p dc h", p=128))
                inc(i, wch, 16)

            @blk.tensor
            def _(t, pi=pi, hc=hc, cbuf=cbuf, ldv=wch.v):
                t.wait_ge(wch.h, ldv)
                hcg = hc % 2
                if hc >= 2:
                    t.wait_ge(sa.h, mm_sa[(pi, hc - 2)])
                    t.wait_ge(a_done.h, mm_adone[(pi, hc - 2)])
                elif pi >= 1:
                    t.wait_ge(b_ve.h, mm_bve[(pi - 1, NT - 1)])
                base = 4 * hcg
                for half in range(2):
                    for dc in range(NDC):
                        mm = t.matmul(ps[base + half][:, :], w1c[:, cbuf, dc, :],
                                      xnT[:, dc, half * 512:(half + 1) * 512],
                                      start=(dc == 0), stop=(dc == NDC - 1))
                    inc(mm, a_pe, 1)
                for half in range(2):
                    for dc in range(NDC):
                        mm = t.matmul(ps[base + 2 + half][:, :], w3c[:, cbuf, dc, :],
                                      xnT[:, dc, half * 512:(half + 1) * 512],
                                      start=(dc == 0), stop=(dc == NDC - 1))
                    inc(mm, a_pe, 1)
                mm_apex[(pi, hc)] = a_pe.v

            @blk.scalar
            def _(sc, pi=pi, hc=hc, pev=a_pe.v):
                hcg = hc % 2
                base = 4 * hcg
                if hc >= 2:
                    sc.wait_ge(a_done.h, mm_adone[(pi, hc - 2)])
                elif pi >= 1:
                    pl = nic - 1 if (nic - 1) % 2 == hc % 2 else nic - 2
                    sc.wait_ge(a_done.h, mm_adone[(pi - 1, pl)])
                sc.wait_ge(a_pe.h, pev - 3)
                sc.activation(silu_t[:, hc % 2, 0:512], ps[base][:, :], AF.Silu)
                sc.wait_ge(a_pe.h, pev - 2)
                i = sc.activation(silu_t[:, hc % 2, 512:1024], ps[base + 1][:, :], AF.Silu)
                inc(i, sa, 2)
                mm_sa[(pi, hc)] = sa.v

            @blk.vector
            def _(v, pi=pi, hc=hc, abuf=abuf, pev=a_pe.v, sav=sa.v):
                hcg = hc % 2
                base = 4 * hcg
                if hc == 0 and pi >= 2:
                    v.wait_ge(b_pe.h, mm_bpe[pi - 2])
                v.wait_ge(sa.h, sav)
                v.wait_ge(a_pe.h, pev - 1)
                v.tensor_tensor(a_sb[:, abuf, hc, 0:512], silu_t[:, hc % 2, 0:512],
                                ps[base + 2][:, :], ALU.mult)
                v.wait_ge(a_pe.h, pev)
                i = v.tensor_tensor(a_sb[:, abuf, hc, 512:1024], silu_t[:, hc % 2, 512:1024],
                                    ps[base + 3][:, :], ALU.mult)
                inc(i, a_done, 2)
                mm_adone[(pi, hc)] = a_done.v

        @blk.gpsimd
        def _(g, pi=pi):
            if pi >= 1:
                g.wait_ge(b_pe.h, mm_bpe[pi - 1])
            i = g.dma_start(out=wC[:, :, :],
                            in_=w2_ap.rearrange("(hc p) d -> p hc d", p=128))
            inc(i, w2_ld, 16)

        for tt in range(NT):
            ttg = tt % 2
            base = 4 * ttg

            @blk.tensor
            def _(t, pi=pi, tt=tt, abuf=abuf, w2v=w2_ld.v):
                if tt == 0:
                    t.wait_ge(w2_ld.h, w2v)
                if tt < 2:
                    t.wait_ge(a_done.h, mm_adone[(pi, nic - 1)])
                    t.wait_ge(sa.h, mm_sa[(pi, nic - 1)])
                else:
                    t.wait_ge(b_ve.h, mm_bve[(pi, tt - 2)])
                for hc in range(nic):
                    for db in range(4):
                        mm = t.matmul(ps[base + db][:, :],
                                      a_sb[:, abuf, hc, tt * 128:(tt + 1) * 128],
                                      wC[:, hc, db * 512:(db + 1) * 512],
                                      start=(hc == 0), stop=(hc == nic - 1))
                inc(mm, b_pe, 1)
                if tt == NT - 1:
                    mm_bpe[pi] = b_pe.v

            @blk.vector
            def _(v, pi=pi, tt=tt, gi=gi, pev=b_pe.v):
                v.wait_ge(b_pe.h, pev)
                for db in range(4):
                    dst = acc[:, tt, db * 512:(db + 1) * 512]
                    src = ps[base + db][:, :]
                    if pi == 0:
                        if gi is None:
                            i = v.tensor_copy(dst, src)
                        else:
                            i = v.tensor_scalar(dst, src, gates[:, tt, gi:gi + 1], None,
                                                ALU.mult)
                    else:
                        sc1 = 1.0 if gi is None else gates[:, tt, gi:gi + 1]
                        i = v.scalar_tensor_tensor(dst, src, sc1, dst, ALU.mult, ALU.add)
                inc(i, b_ve, 1)
                mm_bve[(pi, tt)] = b_ve.v

    return mm_bve[(npx - 1, NT - 1)]



def build_l0ffn():
    NE = 8
    NIC = 11
    ctx = Ctx()
    nc = ctx.nc

    xn_bf = nc.dram_tensor("xn_bf", [TPC, D], BF16, kind="ExternalInput").ap()
    x = nc.dram_tensor("x", [TPC, D], F32, kind="ExternalInput").ap()
    s_bf = nc.dram_tensor("s_bf", [TPC, D], BF16, kind="ExternalInput").ap()
    gwT = nc.dram_tensor("gwT", [D, NE], F32, kind="ExternalInput").ap()
    rb = nc.dram_tensor("rb", [128, NE], F32, kind="ExternalInput").ap()
    W1 = nc.dram_tensor("W1", [NE, D, 1408], F32, kind="ExternalInput").ap()
    W3 = nc.dram_tensor("W3", [NE, D, 1408], F32, kind="ExternalInput").ap()
    W2 = nc.dram_tensor("W2", [NE, 1408, D], F32, kind="ExternalInput").ap()
    sw1T = nc.dram_tensor("sw1T", [D, 2816], F32, kind="ExternalInput").ap()
    sw3T = nc.dram_tensor("sw3T", [D, 2816], F32, kind="ExternalInput").ap()
    sw2T = nc.dram_tensor("sw2T", [2816, D], F32, kind="ExternalInput").ap()
    ident = nc.dram_tensor("ident", [128, 128], F32, kind="ExternalInput").ap()

    x1 = nc.dram_tensor("x1", [TPC, D], F32, kind="ExternalOutput").ap()
    dbg_acc = nc.dram_tensor("dbg_acc", [TPC, D], F32, kind="ExternalOutput").ap()
    dbg_gates = nc.dram_tensor("dbg_gates", [128, NT, NE], F32, kind="ExternalOutput").ap()
    dbg_rt = nc.dram_tensor("dbg_rt", [128, NT, 8, NE], F32, kind="ExternalOutput").ap()

    xnT = ctx.sbuf("xnT", [128, NDC, TPC], BF16)
    id_bf = ctx.sbuf("id_bf", [128, 128], BF16)
    acc = ctx.sbuf("acc", [128, NT, D], F32)
    gates = ctx.sbuf("gates", [128, NT, NE], F32)
    gw_sb = ctx.sbuf("gw_sb", [128, NDC, NE], BF16)
    rb_sb = ctx.sbuf("rb_sb", [128, NE], F32)
    rt = ctx.sbuf("rt", [128, 6, NE], F32)
    rs = ctx.sbuf("rs", [128, 8], F32)
    rdbg = ctx.sbuf("rdbg", [128, NT, 8, NE], F32)

    ps = [ctx.psum(f"ps{i}", [128, 512]) for i in range(8)]

    id_ld = ctx.sem("id_ld")
    gw_ld = ctx.sem("gw_ld")
    r_pe = ctx.sem("r_pe")
    r_done = ctx.sem("r_done")
    r_vf = ctx.sem("r_vf")
    sems = {k: ctx.sem(k) for k in ("wch", "a_pe", "sa", "a_done", "w2_ld", "b_pe", "b_ve")}

    blk = ctx.st.enter_context(nc.Block())

    @blk.gpsimd
    def _(g):
        i = g.dma_start(out=id_bf[:, :], in_=ident[:, :])
        inc(i, id_ld, 16)
        i = g.dma_start(out=gw_sb[:, :, :], in_=gwT.rearrange("(dc p) e -> p dc e", p=128))
        inc(i, gw_ld, 16)
        i = g.dma_start(out=rb_sb[:, :], in_=rb[:, :])
        inc(i, gw_ld, 16)

    tp_stack = contextlib.ExitStack()
    tp_done = emit_tp(ctx, blk, xn_bf, xnT, id_bf, [ps[0], ps[1]], "tp", (id_ld, 16),
                      stack=tp_stack)
    tp_stack.close()

    for tt in range(NT):
        @blk.tensor
        def _(t, tt=tt):
            if tt == 0:
                t.wait_ge(gw_ld.h, 32)
                t.wait_ge(tp_done.h, 4 * NT)
            else:
                t.wait_ge(r_done.h, 2 * tt - 1)
            for dc in range(NDC):
                mm = t.matmul(ps[2][:, 0:NE], xnT[:, dc, tt * 128:(tt + 1) * 128],
                              gw_sb[:, dc, :], start=(dc == 0), stop=(dc == NDC - 1))
            inc(mm, r_pe, 1)

        @blk.scalar
        def _(sc, tt=tt, pev=r_pe.v):
            sc.wait_ge(r_pe.h, pev)
            if tt >= 1:
                sc.wait_ge(r_done.h, 2 * tt)
            i = sc.activation(rt[:, 0, :], ps[2][:, 0:NE], AF.Sigmoid)
            inc(i, r_done, 1)

        @blk.vector
        def _(v, tt=tt, rdv=r_done.v):
            v.wait_ge(r_done.h, rdv)
            sc0 = rt[:, 0, :]
            selb = rt[:, 1, :]
            m1 = rs[:, 0:1]
            mask = rt[:, 2, :]
            tmp = rt[:, 3, :]
            i = v.tensor_tensor(selb, sc0, rb_sb[:, :], ALU.add)
            fence(v, i, r_vf)
            i = v.reduce_max(m1, selb, axis=AX.X)
            fence(v, i, r_vf)
            i = v.tensor_scalar(mask, selb, m1, None, ALU.is_equal)
            fence(v, i, r_vf)
            i = v.scalar_tensor_tensor(tmp, mask, 1.0, sc0, ALU.mult, ALU.mult,
                                       accum_out=rs[:, 1:2])
            fence(v, i, r_vf)
            i = v.scalar_tensor_tensor(selb, mask, -1e30, selb, ALU.mult, ALU.add)
            fence(v, i, r_vf)
            i = v.tensor_copy(rt[:, 4, :], mask)
            fence(v, i, r_vf)
            i = v.reduce_max(m1, selb, axis=AX.X)
            fence(v, i, r_vf)
            i = v.tensor_scalar(mask, selb, m1, None, ALU.is_equal)
            fence(v, i, r_vf)
            i = v.scalar_tensor_tensor(tmp, mask, 1.0, sc0, ALU.mult, ALU.mult,
                                       accum_out=rs[:, 2:3])
            fence(v, i, r_vf)
            i = v.tensor_copy(rs[:, 6:7], rs[:, 0:1])
            fence(v, i, r_vf)
            i = v.tensor_tensor(rs[:, 3:4], rs[:, 1:2], rs[:, 2:3], ALU.add)
            fence(v, i, r_vf)
            i = v.tensor_scalar(rs[:, 3:4], rs[:, 3:4], 1e-12, None, ALU.max)
            fence(v, i, r_vf)
            i = v.reciprocal(rs[:, 4:5], rs[:, 3:4])
            fence(v, i, r_vf)
            i = v.tensor_tensor(mask, mask, rt[:, 4, :], ALU.add)
            fence(v, i, r_vf)
            i = v.tensor_tensor(tmp, mask, sc0, ALU.mult)
            fence(v, i, r_vf)
            i = v.tensor_scalar(gates[:, tt, :], tmp, rs[:, 4:5], None, ALU.mult)
            inc(i, r_done, 1)

    pexps = [(W1[e], W3[e], W2[e], e) for e in range(NE)]
    pexps.append((sw1T[:, 0:1408], sw3T[:, 0:1408], sw2T[0:1408, :], None))
    pexps.append((sw1T[:, 1408:2816], sw3T[:, 1408:2816], sw2T[1408:2816, :], None))

    @blk.tensor
    def _(t):
        t.wait_ge(r_done.h, 2 * NT)

    ex_stack = contextlib.ExitStack()
    emit_experts(ctx, blk, xnT, acc, gates, pexps, ps, sems, NIC, NIC, "ex",
                 stack=ex_stack, after=[(tp_done, 4 * NT)])
    ex_stack.close()
    b_ve = sems["b_ve"]
    acc_total = b_ve.v
    bpe_total = sems["b_pe"].v

    s_t = ctx.sbuf("s_t", [128, 2, D], BF16)
    x_t = ctx.sbuf("x_t", [128, 2, D], F32)
    g_t = ctx.sbuf("g_t", [128, 2, D], BF16)
    sg_t = ctx.sbuf("sg_t", [128, 2, D], F32)
    f_ld = ctx.sem("f_ld")
    f_si = ctx.sem("f_si")
    f_ve = ctx.sem("f_ve")
    f_st = ctx.sem("f_st")

    for tt in range(NT):
        buf = tt % 2

        @blk.sync
        def _(s, tt=tt, buf=buf):
            if tt == 0:
                s.wait_ge(sems["b_pe"].h, bpe_total)
            if tt >= 2:
                s.wait_ge(f_st.h, 16 * (tt - 1))
            i = s.dma_start(out=s_t[:, buf, :], in_=s_bf[tt * 128:(tt + 1) * 128, :])
            inc(i, f_ld, 16)
            i = s.dma_start(out=x_t[:, buf, :], in_=x[tt * 128:(tt + 1) * 128, :])
            inc(i, f_ld, 16)

        @blk.scalar
        def _(sc, tt=tt, buf=buf):
            sc.wait_ge(b_ve.h, acc_total)
            if tt >= 2:
                sc.wait_ge(f_ve.h, 2 * (tt - 1))
            i = sc.activation(g_t[:, buf, :], acc[:, tt, :], AF.Silu)
            inc(i, f_si, 1)

        @blk.vector
        def _(v, tt=tt, buf=buf, ldv=f_ld.v, siv=f_si.v):
            v.wait_ge(f_ld.h, ldv)
            v.wait_ge(f_si.h, siv)
            v.tensor_tensor(sg_t[:, buf, :], g_t[:, buf, :], s_t[:, buf, :], ALU.mult)
            i = v.tensor_tensor(x_t[:, buf, :], x_t[:, buf, :], sg_t[:, buf, :], ALU.add)
            inc(i, f_ve, 2)

        @blk.sync
        def _(s, tt=tt, buf=buf, vev=f_ve.v):
            s.wait_ge(f_ve.h, vev)
            i = s.dma_start(out=x1[tt * 128:(tt + 1) * 128, :], in_=x_t[:, buf, :])
            inc(i, f_st, 16)

    @blk.sync
    def _(s):
        s.wait_ge(b_ve.h, acc_total)
        s.wait_ge(r_done.h, 2 * NT)
        i = s.dma_start(out=dbg_acc.rearrange("(tt p) d -> p tt d", p=128), in_=acc[:, :, :])
        inc(i, f_st, 16)
        i = s.dma_start(out=dbg_gates[:, :, :], in_=gates[:, :, :])
        inc(i, f_st, 16)
        i = s.dma_start(out=dbg_rt[:, :, :, :], in_=rdbg[:, :, :, :])
        inc(i, f_st, 16)
        s.wait_ge(f_st.h, f_st.v)

    return ctx.close()



def build_l1pre():
    ctx = Ctx()
    nc = ctx.nc

    x1 = nc.dram_tensor("x1", [TPC, D], F32, kind="ExternalInput").ap()
    wq = nc.dram_tensor("wq", [D, D], F32, kind="ExternalInput").ap()
    wk = nc.dram_tensor("wk", [D, 512], F32, kind="ExternalInput").ap()
    wv = nc.dram_tensor("wv", [D, 512], F32, kind="ExternalInput").ap()
    qcos = nc.dram_tensor("qcos", [TPC, 128], F32, kind="ExternalInput").ap()
    qssin = nc.dram_tensor("qssin", [TPC, 128], F32, kind="ExternalInput").ap()
    kcos = nc.dram_tensor("kcos", [TPC, 128], F32, kind="ExternalInput").ap()
    kssin = nc.dram_tensor("kssin", [TPC, 128], F32, kind="ExternalInput").ap()
    ident = nc.dram_tensor("ident", [128, 128], F32, kind="ExternalInput").ap()

    xn1_bf = nc.dram_tensor("xn1_bf", [TPC, D], BF16, kind="ExternalOutput").ap()
    qr = nc.dram_tensor("qr", [TPC, D], BF16, kind="ExternalOutput").ap()
    kr = nc.dram_tensor("kr", [TPC, 512], BF16, kind="ExternalOutput").ap()
    vv = nc.dram_tensor("vv", [TPC, 512], BF16, kind="ExternalOutput").ap()

    xnT = ctx.sbuf("xnT", [128, NDC, TPC], BF16)
    xnbf = ctx.sbuf("xnbf", [128, NT, D], BF16)
    id_bf = ctx.sbuf("id_bf", [128, 128], BF16)
    w_sb = ctx.sbuf("w_sb", [128, 2, NDC, 512], BF16)
    qf32 = ctx.sbuf("qf32", [128, 2, 512], F32)
    rtmp = ctx.sbuf("rtmp", [128, 4, 3, 128], F32)
    qr_sb = ctx.sbuf("qr_sb", [128, NT, D], BF16)
    kr_sb = ctx.sbuf("kr_sb", [128, NT, 512], BF16)
    vv_sb = ctx.sbuf("vv_sb", [128, NT, 512], BF16)
    tabs = ctx.sbuf("tabs", [128, 4, NT, 128], F32)

    psT = [ctx.psum(f"psT{i}", [128, 4, 128]) for i in range(2)]
    psq = [ctx.psum(f"psq{i}", [128, 512]) for i in range(2)]

    id_ld = ctx.sem("id_ld")
    tab_ld = ctx.sem("tab_ld")
    qw_ld = ctx.sem("qw_ld")
    qk_pe = ctx.sem("qk_pe")
    qk_sc = ctx.sem("qk_sc")
    qk_ve = ctx.sem("qk_ve")
    st = ctx.sem("st")

    blk = ctx.st.enter_context(nc.Block())

    @blk.gpsimd
    def _(g):
        i = g.dma_start(out=id_bf[:, :], in_=ident[:, :])
        inc(i, id_ld, 16)

    @blk.scalar
    def _(sc):
        for ti, tab in enumerate((qcos, qssin, kcos, kssin)):
            i = sc.dma_start(out=tabs[:, ti, :, :],
                             in_=tab.rearrange("(tt p) i -> p tt i", p=128))
            inc(i, tab_ld, 16)

    xn_done, tp_done = emit_ln(ctx, blk, x1, xnT, id_bf, psT, "l1", xnbf=xnbf,
                               id_ld=(id_ld, 16))

    @blk.sync
    def _(s):
        s.wait_ge(xn_done.h, NT)
        i = s.dma_start(out=xn1_bf.rearrange("(tt p) d -> p tt d", p=128), in_=xnbf[:, :, :])
        inc(i, st, 16)

    specs = [("q", wq, eb) for eb in range(4)] + [("k", wk, 0), ("v", wv, 0)]
    sc_ms = {}
    ve_ms = {}
    for bi, (kind, w_ap, eb) in enumerate(specs):
        buf = bi % 2

        @blk.gpsimd
        def _(g, bi=bi, buf=buf, kind=kind, w_ap=w_ap, eb=eb):
            if bi >= 2:
                g.wait_ge(qk_pe.h, (bi - 1) * NT)
            i = g.dma_start(out=w_sb[:, buf, :, :],
                            in_=w_ap[:, eb * 512:(eb + 1) * 512]
                            .rearrange("(dc p) e -> p dc e", p=128))
            inc(i, qw_ld, 16)

        for tt in range(NT):
            gidx = bi * NT + tt
            pbuf = gidx % 2

            @blk.tensor
            def _(t, bi=bi, buf=buf, tt=tt, gidx=gidx, pbuf=pbuf, ldv=qw_ld.v):
                t.wait_ge(qw_ld.h, ldv)
                if bi == 0 and tt == 0:
                    t.wait_ge(tp_done.h, 4 * NT)
                if gidx >= 2:
                    t.wait_ge(qk_sc.h, sc_ms[gidx - 2])
                for dc in range(NDC):
                    mm = t.matmul(psq[pbuf][:, :], xnT[:, dc, tt * 128:(tt + 1) * 128],
                                  w_sb[:, buf, dc, :],
                                  start=(dc == 0), stop=(dc == NDC - 1))
                inc(mm, qk_pe, 1)

            @blk.scalar
            def _(sc, gidx=gidx, pbuf=pbuf, pev=qk_pe.v):
                sc.wait_ge(qk_pe.h, pev)
                sc.wait_ge(tab_ld.h, 64)
                if gidx >= 2:
                    sc.wait_ge(qk_ve.h, ve_ms[gidx - 2] if (gidx - 2) in ve_ms else 0)
                i = sc.copy(qf32[:, pbuf, :], psq[pbuf][:, :])
                inc(i, qk_sc, 1)
                sc_ms[gidx] = qk_sc.v

            @blk.vector
            def _(v, kind=kind, eb=eb, tt=tt, gidx=gidx, pbuf=pbuf, scv=qk_sc.v):
                v.wait_ge(qk_sc.h, scv)
                if kind == "v":
                    i = v.tensor_copy(vv_sb[:, tt, :], qf32[:, pbuf, :])
                else:
                    ci = 0 if kind == "q" else 2
                    dst = qr_sb[:, tt, eb * 512:(eb + 1) * 512] if kind == "q" \
                        else kr_sb[:, tt, :]
                    for h in range(4):
                        c0 = h * 128
                        v.tensor_copy(rtmp[:, h, 0, 0:64], qf32[:, pbuf, c0 + 64:c0 + 128])
                        v.tensor_copy(rtmp[:, h, 0, 64:128], qf32[:, pbuf, c0:c0 + 64])
                    for h in range(4):
                        c0 = h * 128
                        v.tensor_tensor(rtmp[:, h, 1, :], qf32[:, pbuf, c0:c0 + 128],
                                        tabs[:, ci, tt, :], ALU.mult)
                    for h in range(4):
                        v.tensor_tensor(rtmp[:, h, 2, :], rtmp[:, h, 0, :],
                                        tabs[:, ci + 1, tt, :], ALU.mult)
                    for h in range(4):
                        c0 = h * 128
                        i = v.tensor_tensor(dst[:, c0:c0 + 128], rtmp[:, h, 1, :],
                                            rtmp[:, h, 2, :], ALU.add)
                inc(i, qk_ve, 1)
                ve_ms[gidx] = qk_ve.v

    @blk.sync
    def _(s):
        s.wait_ge(qk_ve.h, qk_ve.v)
        i = s.dma_start(out=qr.rearrange("(tt p) d -> p tt d", p=128), in_=qr_sb[:, :, :])
        inc(i, st, 16)
        i = s.dma_start(out=kr.rearrange("(tt p) d -> p tt d", p=128), in_=kr_sb[:, :, :])
        inc(i, st, 16)
        i = s.dma_start(out=vv.rearrange("(tt p) d -> p tt d", p=128), in_=vv_sb[:, :, :])
        inc(i, st, 16)
        s.wait_ge(st.h, st.v)

    return ctx.close()



def emit_tp_sb(ctx, blk, src_sb, dstT, id_bf, psA2, pfx, waits, done_each,
               pe_sem, ve_sem):
    psv = [p.rearrange("p (a b) -> p a b", a=4) for p in psA2]
    base_pe = pe_sem.v
    base_ve = ve_sem.v
    for tt in range(NT):
        @blk.tensor
        def _(t, tt=tt):
            if tt == 0:
                for (sm, val) in waits:
                    t.wait_ge(sm.h, val)
            for g in range(4):
                gid = 4 * tt + g
                if gid >= 2:
                    t.wait_ge(ve_sem.h, base_ve + gid - 1)
                pst = psv[gid % 2]
                for k_ in range(4):
                    dc = g * 4 + k_
                    mm = t.matmul(pst[:, k_, :], src_sb[:, tt, dc * 128:(dc + 1) * 128],
                                  id_bf[:, :], start=True, stop=True)
                inc(mm, pe_sem, 1)

        @blk.vector
        def _(v, tt=tt):
            for g in range(4):
                gid = 4 * tt + g
                v.wait_ge(pe_sem.h, base_pe + gid + 1)
                i = v.tensor_copy(dstT[:, g * 4:(g + 1) * 4, tt * 128:(tt + 1) * 128],
                                  psv[gid % 2][:, :, :])
                inc(i, ve_sem, 1)



def build_l1attn():
    NKT = 32
    ctx = Ctx()
    nc = ctx.nc

    qr = nc.dram_tensor("qr", [TPC, D], BF16, kind="ExternalInput").ap()
    Kb = nc.dram_tensor("Kb", [SEQ, 512], BF16, kind="ExternalInput").ap()
    Vb = nc.dram_tensor("Vb", [SEQ, 512], BF16, kind="ExternalInput").ap()
    wo = nc.dram_tensor("wo", [D, D], F32, kind="ExternalInput").ap()
    dmask = nc.dram_tensor("dmask", [4, 128, 128], F32, kind="ExternalInput").ap()
    ident = nc.dram_tensor("ident", [128, 128], F32, kind="ExternalInput").ap()

    s1 = nc.dram_tensor("s1", [TPC, D], BF16, kind="ExternalOutput").ap()

    id_bf = ctx.sbuf("id_bf", [128, 128], BF16)
    dm_sb = ctx.sbuf("dm_sb", [128, 4, 128], F32)
    s1_sb = ctx.sbuf("s1_sb", [128, NT, D], BF16)

    ps = [ctx.psum(f"ps{i}", [128, 512]) for i in range(8)]
    psPT = [ps[2], ps[3]]

    a_ld = ctx.sem("a_ld")
    tq_pe = ctx.sem("tq_pe")
    tq_ve = ctx.sem("tq_ve")
    s_pe = ctx.sem("s_pe")
    s_ve = ctx.sem("s_ve")
    x_sc = ctx.sem("x_sc")
    pt_pe = ctx.sem("pt_pe")
    pt_ve = ctx.sem("pt_ve")
    o_pe = ctx.sem("o_pe")
    o_ve = ctx.sem("o_ve")
    af = ctx.sem("af")
    dv_sc = ctx.sem("dv_sc")
    st = ctx.sem("st")

    blk = ctx.st.enter_context(nc.Block())

    att_stack = contextlib.ExitStack()
    qfm = ctx.sbuf("qfm", [128, 16, TPC], BF16, stack=att_stack)
    v_t = ctx.sbuf("v_t", [128, NKT, 512], BF16, stack=att_stack)
    kfm = ctx.sbuf("kfm", [128, 4, SEQ], BF16, stack=att_stack)

    @blk.sync
    def _(s):
        i = s.dma_start(out=v_t[:, :, :], in_=Vb.rearrange("(kt p) e -> p kt e", p=128))
        inc(i, a_ld, 16)

    @blk.gpsimd
    def _(g):
        i = g.dma_start(out=id_bf[:, :], in_=ident[:, :])
        inc(i, a_ld, 16)
        i = g.dma_start(out=dm_sb[:, :, :], in_=dmask.rearrange("j p e -> p j e"))
        inc(i, a_ld, 16)

    kt_stack = contextlib.ExitStack()
    k_t = ctx.sbuf("k_t", [128, NKT, 512], BF16, stack=kt_stack)

    @blk.sync
    def _(s):
        i = s.dma_start(out=k_t[:, :, :], in_=Kb.rearrange("(kt p) e -> p kt e", p=128))
        inc(i, a_ld, 16)
    kld = a_ld.v

    for gi in range(NKT):
        kvh, kap = divmod(gi, 8)

        @blk.tensor
        def _(t, gi=gi, kvh=kvh, kap=kap):
            if gi == 0:
                t.wait_ge(a_ld.h, kld)
            if gi >= 2:
                t.wait_ge(tq_ve.h, gi - 1)
            pst = psPT[gi % 2].rearrange("p (a b) -> p a b", a=4)
            for j in range(4):
                rt = 8 * j + kap
                mm = t.matmul(pst[:, j, :], k_t[:, rt, kvh * 128:(kvh + 1) * 128],
                              id_bf[:, :], start=True, stop=True)
            inc(mm, tq_pe, 1)

        @blk.vector
        def _(v, gi=gi, kvh=kvh, kap=kap):
            v.wait_ge(tq_pe.h, gi + 1)
            pst = psPT[gi % 2]
            i = v.tensor_copy(kfm[:, kvh, kap * 512:(kap + 1) * 512], pst[:, :])
            inc(i, tq_ve, 1)

    ktp_pe = tq_pe.v
    ktp_ve = tq_ve.v
    kt_stack.close()

    qtp_stack = contextlib.ExitStack()
    qtp_done = emit_tp(ctx, blk, qr, qfm, id_bf, [ps[0], ps[1]], "tq",
                       None, stack=qtp_stack, after=[(tq_pe, ktp_pe)])
    qtp_stack.close()

    S_sb = ctx.sbuf("S_sb", [128, 2, SEQ], F32, stack=att_stack)
    P_sb = ctx.sbuf("P_sb", [128, 2, SEQ], BF16, stack=att_stack)
    PT_sb = ctx.sbuf("PT_sb", [128, 2, NKT, 128], BF16, stack=att_stack)
    stat = ctx.sbuf("stat", [128, 2, 24], F32, stack=att_stack)
    tpA_ve = ktp_ve

    pairs = [(i, h) for i in range(NT) for h in range(16)]
    ms_sve = {}
    ms_xsc = {}
    ms_ptpe = {}
    ms_ptve = {}
    ms_ope = {}
    ms_ove = {}
    ms_skappa = {}
    gpt = 0

    for n, (qi, h) in enumerate(pairs):
        nb = qi + 1
        nkt = 4 * nb
        pb = n % 2
        hkv = h // 4

        sve_start = s_ve.v

        @blk.tensor
        def _(t, n=n, qi=qi, h=h, pb=pb, hkv=hkv, nb=nb, sve_start=sve_start):
            if n == 0:
                t.wait_ge(tq_ve.h, tpA_ve)
                t.wait_ge(qtp_done.h, 4 * NT)
            for kap in range(nb):
                if n >= 1 and kap < 2:
                    t.wait_ge(s_ve.h, sve_start)
                if kap >= 2:
                    t.wait_ge(s_ve.h, sve_start + kap - 1)
                mm = t.matmul(ps[kap % 2][:, :], qfm[:, h, qi * 128:(qi + 1) * 128],
                              kfm[:, hkv, kap * 512:(kap + 1) * 512],
                              start=True, stop=True)
                inc(mm, s_pe, 1)

        @blk.vector
        def _(v, n=n, qi=qi, pb=pb, nb=nb, pev=s_pe.v):
            base = pev - nb
            if n >= 2:
                v.wait_ge(x_sc.h, ms_xsc[n - 2])
            for kap in range(nb):
                v.wait_ge(s_pe.h, base + kap + 1)
                pbank = ps[kap % 2]
                if kap == nb - 1:
                    for j in range(4):
                        v.tensor_tensor(pbank[:, j * 128:(j + 1) * 128],
                                        pbank[:, j * 128:(j + 1) * 128],
                                        dm_sb[:, j, :], ALU.add)
                i = v.tensor_scalar(S_sb[:, pb, kap * 512:(kap + 1) * 512],
                                    pbank[:, :], 0.0, None, ALU.add,
                                    ALU.max, accum_out=stat[:, pb, kap:kap + 1])
                inc(i, s_ve, 1)
                ms_skappa[(n, kap)] = s_ve.v
            ms_sve[n] = s_ve.v
            i = v.tensor_copy(stat[:, pb, 20:21], stat[:, pb, 0:1])
            fence(v, i, af)
            i = v.reduce_max(stat[:, pb, 8:9], stat[:, pb, 0:nb], axis=AX.X)
            fence(v, i, af)
            i = v.tensor_scalar(stat[:, pb, 9:10], stat[:, pb, 8:9], -1.0, None, ALU.mult)
            inc(i, dv_sc, 1)

        @blk.scalar
        def _(sc, n=n, pb=pb, nb=nb, dvv=dv_sc.v, svv=s_ve.v):
            sc.wait_ge(dv_sc.h, dvv)
            if n >= 2:
                sc.wait_ge(pt_pe.h, ms_ptpe[n - 2])
            for kap in range(nb):
                i = sc.activation(P_sb[:, pb, kap * 512:(kap + 1) * 512],
                                  S_sb[:, pb, kap * 512:(kap + 1) * 512], AF.Exp,
                                  bias=stat[:, pb, 9:10],
                                  accum_out=stat[:, pb, 10 + kap:11 + kap])
                inc(i, x_sc, 1)
            i = sc.mul(stat[:, pb, 21:22], stat[:, pb, 9:10], 1.0)
            inc(i, x_sc, 1)
            ms_xsc[n] = x_sc.v

        @blk.vector
        def _(v, n=n, pb=pb, nb=nb, xv=x_sc.v):
            v.wait_ge(x_sc.h, xv)
            i = v.reduce_sum(stat[:, pb, 18:19], stat[:, pb, 10:10 + nb], axis=AX.X)
            fence(v, i, af)
            i = v.reciprocal(stat[:, pb, 19:20], stat[:, pb, 18:19])
            fence(v, i, af)

        ptve_start = pt_ve.v

        @blk.tensor
        def _(t, n=n, pb=pb, nkt=nkt, xv=x_sc.v, g0=gpt, ptve_start=ptve_start):
            t.wait_ge(x_sc.h, xv)
            for g in range(nkt // 4):
                gg = g0 + g
                if gg >= 2:
                    if g >= 2:
                        t.wait_ge(pt_ve.h, ptve_start + g - 1)
                    else:
                        t.wait_ge(pt_ve.h, ptve_start)
                pst = psPT[gg % 2].rearrange("p (a b) -> p a b", a=4)
                for k_ in range(4):
                    kt = g * 4 + k_
                    mm = t.matmul(pst[:, k_, :], P_sb[:, pb, kt * 128:(kt + 1) * 128],
                                  id_bf[:, :], start=True, stop=True)
                inc(mm, pt_pe, 1)
            ms_ptpe[n] = pt_pe.v

        @blk.vector
        def _(v, n=n, pb=pb, nkt=nkt, g0=gpt, ptv=pt_pe.v):
            base = ptv - nkt // 4
            if n >= 2:
                v.wait_ge(o_pe.h, ms_ope[n - 2])
            for g in range(nkt // 4):
                gg = g0 + g
                v.wait_ge(pt_pe.h, base + g + 1)
                pst = psPT[gg % 2].rearrange("p (a b) -> p a b", a=4)
                i = v.tensor_copy(PT_sb[:, pb, g * 4:(g + 1) * 4, :], pst[:, :, :])
                inc(i, pt_ve, 1)
                ms_ptve[gg] = pt_ve.v
        gpt += nkt // 4

        @blk.tensor
        def _(t, n=n, pb=pb, hkv=hkv, nkt=nkt, ptv=pt_ve.v):
            if n >= 2:
                t.wait_ge(o_ve.h, ms_ove[n - 2])
            t.wait_ge(pt_ve.h, ptv)
            for kt in range(nkt):
                vidx = 8 * (kt % 4) + kt // 4
                mm = t.matmul(ps[4 + pb][:, 0:128], PT_sb[:, pb, kt, :],
                              v_t[:, vidx, hkv * 128:(hkv + 1) * 128],
                              start=(kt == 0), stop=(kt == nkt - 1))
            inc(mm, o_pe, 1)
            ms_ope[n] = o_pe.v

        @blk.vector
        def _(v, n=n, qi=qi, h=h, pb=pb, ov=o_pe.v):
            v.wait_ge(o_pe.h, ov)
            i = v.tensor_scalar(s1_sb[:, qi, h * 128:(h + 1) * 128], ps[4 + pb][:, 0:128],
                                stat[:, pb, 19:20], None, ALU.mult)
            inc(i, o_ve, 1)
            ms_ove[n] = o_ve.v

    att_end_ove = o_ve.v
    att_stack.close()

    aoT = ctx.sbuf("aoT", [128, NDC, TPC], BF16)
    wo_sb = ctx.sbuf("wo_sb", [128, 2, NDC, 512], BF16)
    so_sb = ctx.sbuf("so_sb", [128, 2, D], BF16)
    w_ld = ctx.sem("w_ld")
    f_pe = ctx.sem("f_pe")
    f_ve = ctx.sem("f_ve")
    f_st = ctx.sem("f_st")

    emit_tp_sb(ctx, blk, s1_sb, aoT, id_bf, [ps[0], ps[1]], "ao",
               [(o_ve, att_end_ove), (s_ve, s_ve.v)], 4, tq_pe, tq_ve)
    ao_ve = tq_ve.v

    for fb in range(4):
        buf = fb % 2

        @blk.gpsimd
        def _(g, fb=fb, buf=buf):
            if fb == 0:
                g.wait_ge(o_ve.h, att_end_ove)
            if fb >= 2:
                g.wait_ge(f_pe.h, 8 * (fb - 1))
            i = g.dma_start(out=wo_sb[:, buf, :, :],
                            in_=wo[:, fb * 512:(fb + 1) * 512]
                            .rearrange("(dc p) e -> p dc e", p=128))
            inc(i, w_ld, 16)

        for tt in range(NT):
            gidx = fb * NT + tt
            pbuf = gidx % 2

            @blk.tensor
            def _(t, fb=fb, buf=buf, tt=tt, gidx=gidx, pbuf=pbuf, ldv=w_ld.v):
                t.wait_ge(w_ld.h, ldv)
                if gidx == 0:
                    t.wait_ge(tq_ve.h, ao_ve)
                if gidx >= 2:
                    t.wait_ge(f_ve.h, gidx - 1)
                for dc in range(NDC):
                    mm = t.matmul(ps[6 + pbuf][:, :], aoT[:, dc, tt * 128:(tt + 1) * 128],
                                  wo_sb[:, buf, dc, :],
                                  start=(dc == 0), stop=(dc == NDC - 1))
                inc(mm, f_pe, 1)

            @blk.vector
            def _(v, fb=fb, tt=tt, gidx=gidx, pbuf=pbuf, pev=f_pe.v):
                v.wait_ge(f_pe.h, pev)
                if gidx >= 2:
                    v.wait_ge(f_st.h, 16 * (gidx - 1))
                i = v.tensor_copy(so_sb[:, gidx % 2, fb * 512:(fb + 1) * 512],
                                  ps[6 + pbuf][:, :])
                inc(i, f_ve, 1)

            @blk.sync
            def _(s, fb=fb, tt=tt, gidx=gidx, vev=f_ve.v):
                s.wait_ge(f_ve.h, vev)
                i = s.dma_start(out=s1[tt * 128:(tt + 1) * 128, fb * 512:(fb + 1) * 512],
                                in_=so_sb[:, gidx % 2, fb * 512:(fb + 1) * 512])
                inc(i, f_st, 16)

    @blk.sync
    def _(s):
        s.wait_ge(f_st.h, f_st.v)

    return ctx.close()



def build_l1ffn():
    NE = 16
    NIC = 8
    ctx = Ctx()
    nc = ctx.nc

    xn_bf = nc.dram_tensor("xn_bf", [TPC, D], BF16, kind="ExternalInput").ap()
    x = nc.dram_tensor("x", [TPC, D], F32, kind="ExternalInput").ap()
    s_bf = nc.dram_tensor("s_bf", [TPC, D], BF16, kind="ExternalInput").ap()
    gwT = nc.dram_tensor("gwT", [D, NE], F32, kind="ExternalInput").ap()
    ebt = nc.dram_tensor("ebt", [128, NE], F32, kind="ExternalInput").ap()
    GT = nc.dram_tensor("GT", [NE, D, 1024], F32, kind="ExternalInput").ap()
    UT = nc.dram_tensor("UT", [NE, D, 1024], F32, kind="ExternalInput").ap()
    DT = nc.dram_tensor("DT", [NE, 1024, D], F32, kind="ExternalInput").ap()
    sgT = nc.dram_tensor("sgT", [D, 1024], F32, kind="ExternalInput").ap()
    suT = nc.dram_tensor("suT", [D, 1024], F32, kind="ExternalInput").ap()
    sdT = nc.dram_tensor("sdT", [1024, D], F32, kind="ExternalInput").ap()
    ident = nc.dram_tensor("ident", [128, 128], F32, kind="ExternalInput").ap()

    out = nc.dram_tensor("out", [TPC, D], F32, kind="ExternalOutput").ap()
    dbg_acc = nc.dram_tensor("dbg_acc", [TPC, D], F32, kind="ExternalOutput").ap()
    dbg_gates = nc.dram_tensor("dbg_gates", [128, NT, NE], F32, kind="ExternalOutput").ap()

    xnT = ctx.sbuf("xnT", [128, NDC, TPC], BF16)
    id_bf = ctx.sbuf("id_bf", [128, 128], BF16)
    acc = ctx.sbuf("acc", [128, NT, D], F32)
    gates = ctx.sbuf("gates", [128, NT, NE], F32)
    gw_sb = ctx.sbuf("gw_sb", [128, NDC, NE], BF16)
    eb_sb = ctx.sbuf("eb_sb", [128, NE], F32)
    rt = ctx.sbuf("rt", [128, 8, NE], F32)
    rs = ctx.sbuf("rs", [128, 16], F32)

    ps = [ctx.psum(f"ps{i}", [128, 512]) for i in range(8)]

    id_ld = ctx.sem("id_ld")
    gw_ld = ctx.sem("gw_ld")
    r_pe = ctx.sem("r_pe")
    r_done = ctx.sem("r_done")
    r_vf = ctx.sem("r_vf")
    sems = {k: ctx.sem(k) for k in ("wch", "a_pe", "sa", "a_done", "w2_ld", "b_pe", "b_ve")}

    blk = ctx.st.enter_context(nc.Block())

    @blk.gpsimd
    def _(g):
        i = g.dma_start(out=id_bf[:, :], in_=ident[:, :])
        inc(i, id_ld, 16)
        i = g.dma_start(out=gw_sb[:, :, :], in_=gwT.rearrange("(dc p) e -> p dc e", p=128))
        inc(i, gw_ld, 16)
        i = g.dma_start(out=eb_sb[:, :], in_=ebt[:, :])
        inc(i, gw_ld, 16)

    tp_stack = contextlib.ExitStack()
    tp_done = emit_tp(ctx, blk, xn_bf, xnT, id_bf, [ps[0], ps[1]], "tp", (id_ld, 16),
                      stack=tp_stack)
    tp_stack.close()

    @blk.vector
    def _(v):
        v.memset(rt[:, 6, 0:4], 1.0)
        v.memset(rt[:, 7, :], -1e30)

    for tt in range(NT):
        @blk.tensor
        def _(t, tt=tt):
            if tt == 0:
                t.wait_ge(gw_ld.h, 32)
                t.wait_ge(tp_done.h, 4 * NT)
            else:
                t.wait_ge(r_done.h, 2 * tt - 1)
            for dc in range(NDC):
                mm = t.matmul(ps[2][:, 0:NE], xnT[:, dc, tt * 128:(tt + 1) * 128],
                              gw_sb[:, dc, :], start=(dc == 0), stop=(dc == NDC - 1))
            inc(mm, r_pe, 1)

        @blk.scalar
        def _(sc, tt=tt, pev=r_pe.v):
            sc.wait_ge(r_pe.h, pev)
            if tt >= 1:
                sc.wait_ge(r_done.h, 2 * tt)
            i = sc.activation(rt[:, 0, :], ps[2][:, 0:NE], AF.Sigmoid)
            inc(i, r_done, 1)

        @blk.vector
        def _(v, tt=tt, rdv=r_done.v):
            v.wait_ge(r_done.h, rdv)
            sc0 = rt[:, 0, :]
            bsd = rt[:, 1, :]
            mk = rt[:, 2, :]
            mb = rt[:, 3, :]
            gacc = rt[:, 5, :]

            def F(i):
                fence(v, i, r_vf)

            F(v.tensor_tensor(bsd, sc0, eb_sb[:, :], ALU.add))
            for g_ in range(4):
                sl = bsd[:, 4 * g_:4 * (g_ + 1)]
                F(v.reduce_max(rs[:, 0:1], sl, axis=AX.X))
                F(v.tensor_scalar(mk[:, 0:4], sl, rs[:, 0:1], None, ALU.is_equal))
                F(v.scalar_tensor_tensor(rt[:, 4, 0:4], mk[:, 0:4], -1e30, sl,
                                         ALU.mult, ALU.add))
                F(v.reduce_max(rs[:, 1:2], rt[:, 4, 0:4], axis=AX.X))
                F(v.tensor_tensor(rs[:, 8 + g_:9 + g_], rs[:, 0:1], rs[:, 1:2], ALU.add))
            gs = rs[:, 8:12]
            F(v.reduce_max(rs[:, 2:3], gs, axis=AX.X))
            F(v.tensor_scalar(rs[:, 12:16], gs, rs[:, 2:3], None, ALU.is_equal))
            F(v.scalar_tensor_tensor(rs[:, 8:12], rs[:, 12:16], -1e30, gs,
                                     ALU.mult, ALU.add))
            F(v.reduce_max(rs[:, 3:4], rs[:, 8:12], axis=AX.X))
            F(v.tensor_scalar(rs[:, 8:12], rs[:, 8:12], rs[:, 3:4], None, ALU.is_equal))
            F(v.tensor_tensor(rs[:, 12:16], rs[:, 12:16], rs[:, 8:12], ALU.add))
            for g_ in range(4):
                F(v.tensor_scalar(rt[:, 4, 4 * g_:4 * (g_ + 1)], rt[:, 6, 0:4],
                                  rs[:, 12 + g_:13 + g_], None, ALU.mult))
            F(v.tensor_tensor(mb, rt[:, 4, :], bsd, ALU.mult))
            F(v.tensor_scalar(rt[:, 4, :], rt[:, 4, :], 1e30, -1e30, ALU.mult, ALU.add))
            F(v.tensor_tensor(mb, mb, rt[:, 4, :], ALU.add))
            F(v.memset(gacc, 0.0))
            for _k in range(4):
                F(v.reduce_max(rs[:, 0:1], mb, axis=AX.X))
                F(v.tensor_scalar(mk, mb, rs[:, 0:1], None, ALU.is_equal))
                F(v.tensor_tensor(rt[:, 4, :], mk, sc0, ALU.mult))
                F(v.tensor_tensor(gacc, gacc, rt[:, 4, :], ALU.add))
                F(v.scalar_tensor_tensor(mb, mk, -1e30, mb, ALU.mult, ALU.add))
            F(v.reduce_sum(rs[:, 4:5], gacc, axis=AX.X))
            F(v.tensor_scalar(rs[:, 4:5], rs[:, 4:5], 1e-20, None, ALU.add))
            F(v.reciprocal(rs[:, 5:6], rs[:, 4:5]))
            i = v.tensor_scalar(gates[:, tt, :], gacc, rs[:, 5:6], 2.5,
                                ALU.mult, ALU.mult)
            inc(i, r_done, 1)

    pexps = [(GT[e], UT[e], DT[e], e) for e in range(NE)]
    pexps.append((sgT, suT, sdT, None))

    @blk.tensor
    def _(t):
        t.wait_ge(r_done.h, 2 * NT)

    ex_stack = contextlib.ExitStack()
    emit_experts(ctx, blk, xnT, acc, gates, pexps, ps, sems, NIC, NIC, "ex",
                 stack=ex_stack, after=[(tp_done, 4 * NT)])
    ex_stack.close()
    b_ve = sems["b_ve"]
    acc_total = b_ve.v
    bpe_total = sems["b_pe"].v

    s_t = ctx.sbuf("s_t", [128, 2, D], BF16)
    x_t = ctx.sbuf("x_t", [128, 2, D], F32)
    g_t = ctx.sbuf("g_t", [128, 2, D], BF16)
    sg_t = ctx.sbuf("sg_t", [128, 2, D], F32)
    f_ld = ctx.sem("f_ld")
    f_si = ctx.sem("f_si")
    f_ve = ctx.sem("f_ve")
    f_st = ctx.sem("f_st")

    for tt in range(NT):
        buf = tt % 2

        @blk.sync
        def _(s, tt=tt, buf=buf):
            if tt == 0:
                s.wait_ge(sems["b_pe"].h, bpe_total)
            if tt >= 2:
                s.wait_ge(f_st.h, 16 * (tt - 1))
            i = s.dma_start(out=s_t[:, buf, :], in_=s_bf[tt * 128:(tt + 1) * 128, :])
            inc(i, f_ld, 16)
            i = s.dma_start(out=x_t[:, buf, :], in_=x[tt * 128:(tt + 1) * 128, :])
            inc(i, f_ld, 16)

        @blk.scalar
        def _(sc, tt=tt, buf=buf):
            sc.wait_ge(b_ve.h, acc_total)
            if tt >= 2:
                sc.wait_ge(f_ve.h, 2 * (tt - 1))
            i = sc.activation(g_t[:, buf, :], acc[:, tt, :], AF.Silu)
            inc(i, f_si, 1)

        @blk.vector
        def _(v, tt=tt, buf=buf, ldv=f_ld.v, siv=f_si.v):
            v.wait_ge(f_ld.h, ldv)
            v.wait_ge(f_si.h, siv)
            v.tensor_tensor(sg_t[:, buf, :], g_t[:, buf, :], s_t[:, buf, :], ALU.mult)
            i = v.tensor_tensor(x_t[:, buf, :], x_t[:, buf, :], sg_t[:, buf, :], ALU.add)
            inc(i, f_ve, 2)

        @blk.sync
        def _(s, tt=tt, buf=buf, vev=f_ve.v):
            s.wait_ge(f_ve.h, vev)
            i = s.dma_start(out=out[tt * 128:(tt + 1) * 128, :], in_=x_t[:, buf, :])
            inc(i, f_st, 16)

    @blk.sync
    def _(s):
        s.wait_ge(b_ve.h, acc_total)
        s.wait_ge(r_done.h, 2 * NT)
        i = s.dma_start(out=dbg_acc.rearrange("(tt p) d -> p tt d", p=128), in_=acc[:, :, :])
        inc(i, f_st, 16)
        i = s.dma_start(out=dbg_gates[:, :, :], in_=gates[:, :, :])
        inc(i, f_st, 16)
        s.wait_ge(f_st.h, f_st.v)

    return ctx.close()



def l0_take(arr, r):
    b, q = divmod(r, 4)
    return np.ascontiguousarray(arr[b, TPC * q:TPC * (q + 1)])


def l1_take(arr, r):
    b, j = divmod(r, 4)
    return np.ascontiguousarray(arr[b, j::4])


def l0_sel(r):
    b, q = divmod(r, 4)
    s = np.zeros((128, NCORES), np.float32)
    s[:, 4 * b:4 * b + q] = 1.0
    return s


def _rope_tables():
    HD = 128
    freqs = 1.0 / 10000.0 ** (np.arange(0, HD, 2, dtype=np.float32) / HD)
    t_ = np.arange(SEQ, dtype=np.float32)
    ang = t_[:, None] * freqs[None, :]
    cos_t = np.tile(np.cos(ang), (1, 2)).astype(np.float32)
    sin_t = np.tile(np.sin(ang), (1, 2)).astype(np.float32)
    ssin_t = np.concatenate([-sin_t[:, :64], sin_t[:, 64:]], axis=1)
    return cos_t, ssin_t


def _dmask(j):
    tq = np.arange(128)[:, None]
    tk = np.arange(128)[None, :]
    return np.stack([np.where(4 * tq + j >= 4 * tk + jk, 0.0, -1e30)
                     .astype(np.float32) for jk in range(4)], axis=0)


def kernel(**inputs):
    inp = {k: np.asarray(v) for k, v in inputs.items()}
    x = inp["L0_attn_x"]
    cores = list(range(NCORES))
    ident = np.eye(128, dtype=np.float32)

    aT = lambda a: np.ascontiguousarray(a.T)
    wqT = aT(inp["L0_attn_Wq"]); wkT = aT(inp["L0_attn_Wk"]); wvT = aT(inp["L0_attn_Wv"])
    nc = build_l0pre()
    im = [{"x": l0_take(x, r), "wqT": wqT, "wkT": wkT, "wvT": wvT, "ident": ident}
          for r in cores]
    r1 = run_bass_kernel_spmd(nc, im, core_ids=cores).results

    gtot = np.stack([np.asarray(r1[r]["totals"]) for r in cores], axis=0)
    woT = aT(inp["L0_attn_Wo"])
    nc = build_l0post()
    im = [{"qfm": r1[r]["qfm"], "ufm": r1[r]["ufm"], "gtot": gtot, "sel": l0_sel(r),
           "woT": woT} for r in cores]
    r2 = run_bass_kernel_spmd(nc, im, core_ids=cores).results

    gwT = aT(inp["L0_ffn_gate_w"])
    rb_t = np.ascontiguousarray(np.tile(inp["L0_ffn_router_bias"][None, :], (128, 1)))
    sw1T = aT(inp["L0_ffn_sw1"]); sw3T = aT(inp["L0_ffn_sw3"]); sw2T = aT(inp["L0_ffn_sw2"])
    nc = build_l0ffn()
    im = [{"xn_bf": r1[r]["xn_bf"], "x": l0_take(x, r), "s_bf": r2[r]["s_bf"],
           "gwT": gwT, "rb": rb_t, "W1": inp["L0_ffn_W1"], "W3": inp["L0_ffn_W3"],
           "W2": inp["L0_ffn_W2"], "sw1T": sw1T, "sw3T": sw3T, "sw2T": sw2T,
           "ident": ident} for r in cores]
    r3 = run_bass_kernel_spmd(nc, im, core_ids=cores).results

    x1_full = np.empty((B, SEQ, D), np.float32)
    for r in cores:
        b, q = divmod(r, 4)
        x1_full[b, TPC * q:TPC * (q + 1)] = np.asarray(r3[r]["x1"])

    cos_t, ssin_t = _rope_tables()
    scale = np.float32(1.0 / np.sqrt(np.float32(128.0)))
    nc = build_l1pre()
    im = []
    for r in cores:
        b, j = divmod(r, 4)
        im.append({"x1": np.ascontiguousarray(x1_full[b, j::4]),
                   "wq": inp["L1_attn_wq"], "wk": inp["L1_attn_wk"],
                   "wv": inp["L1_attn_wv"],
                   "qcos": np.ascontiguousarray(cos_t[j::4] * scale),
                   "qssin": np.ascontiguousarray(ssin_t[j::4] * scale),
                   "kcos": np.ascontiguousarray(cos_t[j::4]),
                   "kssin": np.ascontiguousarray(ssin_t[j::4]),
                   "ident": ident})
    r4 = run_bass_kernel_spmd(nc, im, core_ids=cores).results

    nc = build_l1attn()
    im = []
    for r in cores:
        b, j = divmod(r, 4)
        Kb = np.concatenate([np.asarray(r4[4 * b + jj]["kr"]) for jj in range(4)], axis=0)
        Vb = np.concatenate([np.asarray(r4[4 * b + jj]["vv"]) for jj in range(4)], axis=0)
        im.append({"qr": r4[r]["qr"], "Kb": Kb, "Vb": Vb, "wo": inp["L1_attn_wo"],
                   "dmask": _dmask(j), "ident": ident})
    r5 = run_bass_kernel_spmd(nc, im, core_ids=cores).results

    gwT1 = aT(inp["L1_ffn_gate_weight"])
    ebt = np.ascontiguousarray(np.tile(inp["L1_ffn_expert_bias"][None, :], (128, 1)))
    GT = np.ascontiguousarray(inp["L1_ffn_gate_proj"].transpose(0, 2, 1))
    UT = np.ascontiguousarray(inp["L1_ffn_up_proj"].transpose(0, 2, 1))
    DT = np.ascontiguousarray(inp["L1_ffn_down_proj"].transpose(0, 2, 1))
    sgT = aT(inp["L1_ffn_shared_gate"]); suT = aT(inp["L1_ffn_shared_up"])
    sdT = aT(inp["L1_ffn_shared_down"])
    nc = build_l1ffn()
    im = []
    for r in cores:
        b, j = divmod(r, 4)
        im.append({"xn_bf": r4[r]["xn1_bf"], "x": np.ascontiguousarray(x1_full[b, j::4]),
                   "s_bf": r5[r]["s1"], "gwT": gwT1, "ebt": ebt, "GT": GT, "UT": UT,
                   "DT": DT, "sgT": sgT, "suT": suT, "sdT": sdT, "ident": ident})
    r6 = run_bass_kernel_spmd(nc, im, core_ids=cores).results

    out = np.empty((B, SEQ, D), np.float32)
    for r in cores:
        b, j = divmod(r, 4)
        out[b, j::4] = np.asarray(r6[r]["out"])
    return out
